# version 01
# baseline (speedup 1.0000x reference)
import numpy as np
import ml_dtypes
import concourse.bass as bass
import concourse.mybir as mybir
from concourse.bass_utils import run_bass_kernel_spmd
from contextlib import ExitStack

F32 = mybir.dt.float32
BF16 = mybir.dt.bfloat16
I32 = mybir.dt.int32
AF = mybir.ActivationFunctionType
ALU = mybir.AluOpType
AX = mybir.AxisListType

NCORES = 8
B = 2
T = 4096
D = 2048
NT = T // 128
NSLOT = 8
NOWN = NSLOT * 128
R = 256
HD = 128
NH = 16
NKV = 4
I_FFN = 8192
LN_EPS = 1e-5
NEG = -1.0e30
BF = ml_dtypes.bfloat16


class Prog:
    ENGS = ("sync", "scalar", "vector", "tensor", "gpsimd")

    def __init__(self, nc):
        self.nc = nc
        self.phases = []
        self.cur = None
        self.stack = ExitStack()
        self.sems = {}
        self.dma_counts = {e: 0 for e in self.ENGS}
        self._cum = {e: [] for e in self.ENGS}
        for e in self.ENGS:
            self.sem("dma_" + e)

    def sb(self, name, shape, dt):
        return self.stack.enter_context(self.nc.sbuf_tensor(name, shape, dt))

    def psum(self, name, shape, dt=F32):
        return self.stack.enter_context(self.nc.psum_tensor(name, shape, dt))

    def sem(self, name):
        s = self.stack.enter_context(self.nc.semaphore(name))
        self.sems[name] = s
        return s

    def phase(self):
        if self.phases:
            for e in self.ENGS:
                self._cum[e].append(self.dma_counts[e])
        self.cur = {e: [] for e in self.ENGS}
        self.phases.append(self.cur)

    def on(self, eng, fn):
        self.cur[eng].append(fn)

    def dma(self, eng, out, in_, **kw):
        sem = self.sems["dma_" + eng]
        def fn(e, out=out, in_=in_, kw=kw):
            e.dma_start(out=out, in_=in_, **kw).then_inc(sem, 16)
        self.on(eng, fn)
        self.dma_counts[eng] += 16

    def finalize(self):
        for e in self.ENGS:
            self._cum[e].append(self.dma_counts[e])
        nc = self.nc
        bars = {e: self.sem("bar_" + e) for e in self.ENGS}
        flsh = {e: self.sem("flush_" + e) for e in self.ENGS}
        dsems = {e: self.sems["dma_" + e] for e in self.ENGS}
        phases = self.phases
        cum = self._cum
        with nc.Block() as block:
            for eng_name in self.ENGS:
                dec = getattr(block, eng_name)
                def make_body(eng_name=eng_name):
                    def body(eng):
                        nflush = 0
                        for k, ph in enumerate(phases):
                            if k > 0:
                                for o in self.ENGS:
                                    if o != eng_name:
                                        eng.wait_ge(bars[o], k)
                            last = None
                            for fn in ph[eng_name]:
                                r = fn(eng)
                                if r is not None:
                                    last = r
                            if last is not None:
                                last.then_inc(flsh[eng_name], 1)
                                nflush += 1
                                eng.wait_ge(flsh[eng_name], nflush)
                            c = cum[eng_name][k]
                            prev = cum[eng_name][k - 1] if k > 0 else 0
                            if c > prev:
                                eng.wait_ge(dsems[eng_name], c)
                            eng.sem_inc(bars[eng_name], 1)
                    return body
                dec(make_body())
        self.stack.close()



def emit_ln(p, x_ap, scr, stat, n=D, want_bf=None, want_f32=None):
    inv_n = 1.0 / n
    p.phase()
    p.on("vector", lambda e: e.tensor_reduce(stat[:, 0:1], x_ap, axis=AX.X, op=ALU.add))
    p.on("scalar", lambda e: e.activation(scr, x_ap, AF.Square, accum_out=stat[:, 1:2]))
    p.phase()
    def v2(e):
        e.tensor_scalar(stat[:, 2:3], stat[:, 0:1], inv_n, None, op0=ALU.mult)
        return e.tensor_scalar(stat[:, 3:4], stat[:, 1:2], inv_n, None, op0=ALU.mult)
    p.on("vector", v2)
    p.phase()
    p.on("vector", lambda e: e.tensor_tensor(stat[:, 4:5], stat[:, 2:3], stat[:, 2:3], op=ALU.mult))
    p.phase()
    p.on("vector", lambda e: e.tensor_tensor(stat[:, 5:6], stat[:, 3:4], stat[:, 4:5], op=ALU.subtract))
    p.phase()
    p.on("vector", lambda e: e.tensor_scalar(stat[:, 6:7], stat[:, 5:6], LN_EPS, None, op0=ALU.add))
    p.phase()
    p.on("scalar", lambda e: e.activation(stat[:, 5:6], stat[:, 6:7], AF.Sqrt))
    p.phase()
    p.on("vector", lambda e: e.reciprocal(stat[:, 7:8], stat[:, 5:6]))
    p.phase()
    def v3(e):
        r = None
        if want_f32 is not None:
            r = e.tensor_scalar(want_f32, x_ap, stat[:, 2:3], stat[:, 7:8],
                                op0=ALU.subtract, op1=ALU.mult)
        if want_bf is not None:
            r = e.tensor_scalar(want_bf, x_ap, stat[:, 2:3], stat[:, 7:8],
                                op0=ALU.subtract, op1=ALU.mult)
        return r
    p.on("vector", v3)


def emit_transpose16(p, src_bf, dst_bf, pt0, pt1, ident, nchunks=16):
    for half in range((nchunks + 7) // 8):
        pt = pt0 if (half % 2 == 0) else pt1
        lo = half * 8
        hi = min(lo + 8, nchunks)
        p.phase()
        def t1(e, lo=lo, hi=hi, pt=pt):
            r = None
            for j in range(lo, hi):
                r = e.transpose(pt[:, (j - lo) * 128:(j - lo + 1) * 128],
                                src_bf[:, j * 128:(j + 1) * 128], ident)
            return r
        p.on("tensor", t1)
        p.phase()
        def c1(e, lo=lo, hi=hi, pt=pt):
            return e.tensor_copy(dst_bf[:, lo * 128:hi * 128], pt[:, 0:(hi - lo) * 128])
        p.on("vector", c1)



def build_l0():
    nc = bass.Bass("TRN2", target_bir_lowering=False)
    xf = nc.dram_tensor("xf", [T, D], F32, kind="ExternalInput").ap()
    xo = nc.dram_tensor("xo", [NOWN, D], F32, kind="ExternalInput").ap()
    ws = nc.dram_tensor("ws", [D, R], F32, kind="ExternalInput").ap()
    wa = nc.dram_tensor("wa", [D, R], F32, kind="ExternalInput").ap()
    wot = nc.dram_tensor("wot", [D, D], F32, kind="ExternalInput").ap()
    fw = nc.dram_tensor("fw", [D, D], F32, kind="ExternalInput").ap()
    gam = nc.dram_tensor("gam", [128, D], F32, kind="ExternalInput").ap()
    bet = nc.dram_tensor("bet", [128, D], F32, kind="ExternalInput").ap()
    bob = nc.dram_tensor("bob", [128, D], F32, kind="ExternalInput").ap()
    fbb = nc.dram_tensor("fbb", [128, D], F32, kind="ExternalInput").ap()
    alb = nc.dram_tensor("alb", [128, D], F32, kind="ExternalInput").ap()
    idb = nc.dram_tensor("idb", [128, 128], BF16, kind="ExternalInput").ap()
    kpo = nc.dram_tensor("kpo", [128, 512], F32, kind="ExternalInput").ap()
    qpo = nc.dram_tensor("qpo", [128, 1], F32, kind="ExternalInput").ap()
    x1o = nc.dram_tensor("x1o", [NOWN, D], F32, kind="ExternalOutput").ap()
    xnsp = nc.dram_tensor("xnsp", [T, D], BF16, kind="Internal").ap()
    swtsp = nc.dram_tensor("swtsp", [D, D], BF16, kind="Internal").ap()
    wotsp = nc.dram_tensor("wotsp", [D, D], BF16, kind="Internal").ap()

    p = Prog(nc)
    sb, psum = p.sb, p.psum
    ident = sb("ident", [128, 128], BF16)
    kposc = sb("kposc", [128, 512], F32)
    qpos = sb("qpos", [128, 1], F32)
    gamma_b = sb("gamma_b", [128, D], F32)
    beta_b = sb("beta_b", [128, D], F32)
    bo_b = sb("bo_b", [128, D], F32)
    fba_b = sb("fba_b", [128, D], F32)
    al_b = sb("al_b", [128, D], F32)
    wsb = sb("wsb", [128, 16 * R], BF16)
    wab = sb("wab", [128, 16 * R], BF16)
    actT = sb("actT", [128, 2 * T], BF16)
    xt = sb("xt", [128, D], F32)
    scr = sb("scr", [128, D], F32)
    stat = sb("stat", [128, 16], F32)
    xnb = sb("xnb", [128, D], BF16)
    xnT = sb("xnT", [128, D], BF16)
    stT = sb("stT", [128, 2 * 128], BF16)
    mask_s = sb("mask_s", [128, 512], F32)
    scob = sb("scob", [128, T], F32)
    pbuf = sb("pbuf", [128, T], BF16)
    ptb = sb("ptb", [128, T], BF16)
    vbuf = sb("vbuf", [128, D], BF16)
    attnb = sb("attnb", [128, D], BF16)
    attnT = sb("attnT", [128, D], BF16)
    s_t = sb("s_t", [128, D], F32)
    xo_t = sb("xo_t", [128, D], F32)
    xn_o = sb("xn_o", [128, D], F32)
    a_b = sb("a_b", [128, D], BF16)
    aT = sb("aT", [128, D], BF16)
    wstr = sb("wstr", [128, 16 * 512], BF16)
    u_t = sb("u_t", [128, D], F32)
    stg = xt
    stgb = xnb
    pattn = psum("pattn", [128, D], F32)
    pscore = psum("pscore", [128, 512], F32)
    pmm = psum("pmm", [128, 512], F32)
    pt0 = psum("pt0", [128, 1024], BF16)
    pt1 = psum("pt1", [128, 1024], BF16)

    p.phase()
    p.dma("sync", ident[:], idb)
    p.dma("sync", kposc[:], kpo)
    p.dma("sync", qpos[:], qpo)
    p.dma("sync", gamma_b[:], gam)
    p.dma("sync", beta_b[:], bet)
    p.dma("sync", bo_b[:], bob)
    p.dma("sync", fba_b[:], fbb)
    p.dma("sync", al_b[:], alb)
    p.phase()
    p.on("vector", lambda e: e.tensor_scalar(mask_s[:], kposc[:], qpos[:, 0:1], NEG,
                                             op0=ALU.is_gt, op1=ALU.mult))
    for name, src, dstt in (("ws", ws, wsb), ("wa", wa, wab)):
        for hc in range(16):
            p.phase()
            p.dma("sync", stg[:, 0:R], src[hc * 128:(hc + 1) * 128, :])
            p.phase()
            p.on("vector", (lambda e, hc=hc, dstt=dstt: e.tensor_copy(
                dstt[:, hc * R:(hc + 1) * R], stg[:, 0:R])))
    for hc in range(16):
        p.phase()
        p.dma("sync", stg[:], wot[hc * 128:(hc + 1) * 128, :])
        p.phase()
        p.on("vector", lambda e: e.tensor_copy(stgb[:], stg[:]))
        p.phase()
        p.dma("scalar", wotsp[hc * 128:(hc + 1) * 128, :], stgb[:])
    for oc in range(16):
        p.phase()
        p.dma("sync", stg[:], fw[oc * 128:(oc + 1) * 128, :])
        p.phase()
        p.on("vector", lambda e: e.tensor_reduce(stat[:, 8:9], stg[:], axis=AX.X, op=ALU.add))
        p.phase()
        p.on("vector", lambda e: e.tensor_scalar(stat[:, 9:10], stat[:, 8:9], 1.0 / D, None, op0=ALU.mult))
        p.phase()
        p.on("vector", lambda e: e.tensor_scalar(scr[:], stg[:], stat[:, 9:10], None, op0=ALU.subtract))
        p.phase()
        p.on("scalar", lambda e: e.activation(stgb[:], scr[:], AF.Sign))
        for half in range(2):
            pt = pt0 if half == 0 else pt1
            p.phase()
            def tx(e, half=half, pt=pt):
                r = None
                for j in range(8):
                    r = e.transpose(pt[:, j * 128:(j + 1) * 128],
                                    stgb[:, (half * 8 + j) * 128:(half * 8 + j + 1) * 128], ident[:])
                return r
            p.on("tensor", tx)
            p.phase()
            def cp(e, half=half, pt=pt):
                return e.tensor_copy(attnT[:, half * 1024:(half + 1) * 1024], pt[:])
            p.on("vector", cp)
        p.phase()
        for ic in range(16):
            p.dma("scalar", swtsp[ic * 128:(ic + 1) * 128, oc * 128:(oc + 1) * 128],
                  attnT[:, ic * 128:(ic + 1) * 128])

    for t in range(NT):
        p.phase()
        p.dma("sync", xt[:], xf[t * 128:(t + 1) * 128, :])
        emit_ln(p, xt[:], scr[:], stat, n=D, want_bf=xnb[:], want_f32=None)
        p.phase()
        p.dma("scalar", xnsp[t * 128:(t + 1) * 128, :], xnb[:])
        emit_transpose16(p, xnb, xnT, pt0, pt1, ident[:])
        p.phase()
        def mm_act(e, t=t):
            r = None
            for rh in range(2):
                for hc in range(16):
                    r = e.matmul(pmm[:, rh * 128:(rh + 1) * 128],
                                 wab[:, hc * R + rh * 128: hc * R + (rh + 1) * 128],
                                 xnT[:, hc * 128:(hc + 1) * 128],
                                 start=(hc == 0), stop=(hc == 15))
            return r
        p.on("tensor", mm_act)
        p.phase()
        def ev_act(e, t=t):
            e.tensor_copy(actT[:, t * 128:(t + 1) * 128], pmm[:, 0:128])
            return e.tensor_copy(actT[:, T + t * 128:T + (t + 1) * 128], pmm[:, 128:256])
        p.on("vector", ev_act)

    for s in range(NSLOT):
        nkb = s + 1
        nkt = 4 * nkb
        p.phase()
        p.dma("sync", xo_t[:], xo[s * 128:(s + 1) * 128, :])
        emit_ln(p, xo_t[:], scr[:], stat, n=D, want_bf=xnb[:], want_f32=xn_o[:])
        emit_transpose16(p, xnb, xnT, pt0, pt1, ident[:])
        p.phase()
        def mm_st(e):
            r = None
            for rh in range(2):
                for hc in range(16):
                    r = e.matmul(pmm[:, rh * 128:(rh + 1) * 128],
                                 wsb[:, hc * R + rh * 128: hc * R + (rh + 1) * 128],
                                 xnT[:, hc * 128:(hc + 1) * 128],
                                 start=(hc == 0), stop=(hc == 15))
            return r
        p.on("tensor", mm_st)
        p.phase()
        p.on("scalar", lambda e: e.mul(stT[:], pmm[:, 0:256], 1.0 / 16.0))
        for kb in range(nkb):
            p.phase()
            def mm_sc(e, kb=kb):
                r = None
                for rh in range(2):
                    r = e.matmul(pscore[:],
                                 stT[:, rh * 128:(rh + 1) * 128],
                                 actT[:, rh * T + kb * 512: rh * T + (kb + 1) * 512],
                                 start=(rh == 0), stop=(rh == 1))
                return r
            p.on("tensor", mm_sc)
            p.phase()
            if kb == s:
                p.on("vector", (lambda e, kb=kb: e.tensor_tensor(
                    scob[:, kb * 512:(kb + 1) * 512], pscore[:], mask_s[:], op=ALU.add)))
            else:
                p.on("vector", (lambda e, kb=kb: e.tensor_copy(
                    scob[:, kb * 512:(kb + 1) * 512], pscore[:])))
        p.phase()
        p.on("vector", (lambda e, nkb=nkb: e.tensor_reduce(
            stat[:, 10:11], scob[:, 0:nkb * 512], axis=AX.X, op=ALU.max)))
        p.phase()
        p.on("vector", lambda e: e.tensor_scalar(stat[:, 11:12], stat[:, 10:11], -1.0, None, op0=ALU.mult))
        p.phase()
        p.on("scalar", (lambda e, nkb=nkb: e.activation(
            pbuf[:, 0:nkb * 512], scob[:, 0:nkb * 512], AF.Exp,
            bias=stat[:, 11:12], accum_out=stat[:, 12:13])))
        p.phase()
        p.on("vector", lambda e: e.reciprocal(stat[:, 13:14], stat[:, 12:13]))
        for kb in range(nkb):
            pt = pt0 if (kb % 2 == 0) else pt1
            p.phase()
            def tP(e, kb=kb, pt=pt):
                r = None
                for j in range(4):
                    r = e.transpose(pt[:, j * 128:(j + 1) * 128],
                                    pbuf[:, kb * 512 + j * 128: kb * 512 + (j + 1) * 128],
                                    ident[:])
                return r
            p.on("tensor", tP)
            p.phase()
            p.on("vector", (lambda e, kb=kb, pt=pt: e.tensor_copy(
                ptb[:, kb * 512:(kb + 1) * 512], pt[:, 0:512])))
        for kt in range(nkt):
            p.phase()
            p.dma("sync", vbuf[:], xnsp[kt * 128:(kt + 1) * 128, :])
            p.phase()
            def mm_sv(e, kt=kt, nkt=nkt):
                r = None
                for c in range(4):
                    r = e.matmul(pattn[:, c * 512:(c + 1) * 512],
                                 ptb[:, kt * 128:(kt + 1) * 128],
                                 vbuf[:, c * 512:(c + 1) * 512],
                                 start=(kt == 0), stop=(kt == nkt - 1))
                return r
            p.on("tensor", mm_sv)
        p.phase()
        p.on("vector", lambda e: e.tensor_scalar(attnb[:], pattn[:], stat[:, 13:14], None, op0=ALU.mult))
        emit_transpose16(p, attnb, attnT, pt0, pt1, ident[:])
        for ob in range(4):
            p.phase()
            for ic in range(16):
                p.dma("sync", wstr[:, ic * 512:(ic + 1) * 512],
                      wotsp[ic * 128:(ic + 1) * 128, ob * 512:(ob + 1) * 512])
            p.phase()
            def mm_wo(e, ob=ob):
                r = None
                for ic in range(16):
                    r = e.matmul(pscore[:], attnT[:, ic * 128:(ic + 1) * 128],
                                 wstr[:, ic * 512:(ic + 1) * 512],
                                 start=(ic == 0), stop=(ic == 15))
                return r
            p.on("tensor", mm_wo)
            p.phase()
            p.on("vector", (lambda e, ob=ob: e.tensor_tensor(
                s_t[:, ob * 512:(ob + 1) * 512], pscore[:],
                bo_b[:, ob * 512:(ob + 1) * 512], op=ALU.add)))
        emit_ln(p, xn_o[:], scr[:], stat, n=D, want_bf=None, want_f32=u_t[:])
        p.phase()
        p.on("vector", lambda e: e.tensor_tensor(scr[:], u_t[:], gamma_b[:], op=ALU.mult))
        p.phase()
        p.on("gpsimd", lambda e: e.tensor_tensor(u_t[:], scr[:], beta_b[:], op=ALU.add))
        p.phase()
        p.on("scalar", lambda e: e.activation(a_b[:], u_t[:], AF.Sign))
        emit_transpose16(p, a_b, aT, pt0, pt1, ident[:])
        for ob in range(4):
            p.phase()
            for ic in range(16):
                p.dma("sync", wstr[:, ic * 512:(ic + 1) * 512],
                      swtsp[ic * 128:(ic + 1) * 128, ob * 512:(ob + 1) * 512])
            p.phase()
            def mm_ffn(e, ob=ob):
                r = None
                for ic in range(16):
                    r = e.matmul(pscore[:], aT[:, ic * 128:(ic + 1) * 128],
                                 wstr[:, ic * 512:(ic + 1) * 512],
                                 start=(ic == 0), stop=(ic == 15))
                return r
            p.on("tensor", mm_ffn)
            p.phase()
            p.on("vector", (lambda e, ob=ob: e.tensor_tensor(
                u_t[:, ob * 512:(ob + 1) * 512], pscore[:],
                fba_b[:, ob * 512:(ob + 1) * 512], op=ALU.add)))
            p.phase()
            p.on("gpsimd", (lambda e, ob=ob: e.tensor_tensor(
                u_t[:, ob * 512:(ob + 1) * 512], u_t[:, ob * 512:(ob + 1) * 512],
                al_b[:, ob * 512:(ob + 1) * 512], op=ALU.mult)))
        p.phase()
        p.on("scalar", lambda e: e.activation(xn_o[:], u_t[:], AF.Silu))
        p.phase()
        p.on("vector", lambda e: e.tensor_tensor(scr[:], xn_o[:], s_t[:], op=ALU.mult))
        p.phase()
        p.on("gpsimd", lambda e: e.tensor_tensor(u_t[:], scr[:], xo_t[:], op=ALU.add))
        p.phase()
        p.dma("sync", x1o[s * 128:(s + 1) * 128, :], u_t[:])

    p.phase()
    p.finalize()
    return nc




QA = dict(RS2A=(255 << 23), RS2B=(2 << 23), CLAMPL=(128 << 23), CLAMPH=(130 << 23),
          PREC=(130 << 23), C=120.0, SBSH=(6 << 23))
QW = dict(RS2A=(255 << 23), RS2B=0, CLAMPL=(126 << 23), CLAMPH=(128 << 23),
          PREC=(128 << 23), C=6.0, SBSH=(2 << 23))

EXPMASK = 0x7f800000


def emit_quant(p, src_f32, out_bf, W, qc, s0t, ebmt, pre2t, iqt, ubt, amaxt, ebt, rs2t, sbxt, errt, tmpt):
    nb = W // 32
    src3 = src_f32.rearrange("p (b c) -> p b c", c=32)
    s03 = s0t.rearrange("p (b c) -> p b c", c=32)
    out3 = out_bf.rearrange("p (b c) -> p b c", c=32)
    rs2b = rs2t.bitcast(F32).unsqueeze(2).broadcast_to([128, nb, 32])
    sbxb = sbxt.bitcast(F32).unsqueeze(2).broadcast_to([128, nb, 32])
    p.phase()
    p.on("vector", lambda e: e.tensor_reduce(
        amaxt.unsqueeze(2), src3, axis=AX.X, op=ALU.max, apply_absolute_value=True))
    p.phase()
    p.on("vector", lambda e: e.tensor_scalar(ebt, amaxt.bitcast(I32), EXPMASK, None,
                                             op0=ALU.bitwise_and))
    p.phase()
    def gp1(e):
        e.tensor_scalar(tmpt, ebt, 2 << 23, qc["RS2A"], op0=ALU.max, op1=ALU.subtract)
        return e.tensor_scalar(sbxt, ebt, qc["SBSH"], None, op0=ALU.subtract)
    p.on("gpsimd", gp1)
    p.phase()
    p.on("gpsimd", lambda e: e.tensor_scalar(rs2t, tmpt, -1, qc["RS2B"],
                                             op0=ALU.mult, op1=ALU.add))
    p.phase()
    p.on("vector", lambda e: e.tensor_tensor(s03, src3, rs2b, op=ALU.mult))
    p.phase()
    p.on("vector", lambda e: e.tensor_scalar(ebmt, s0t.bitcast(I32), EXPMASK, None,
                                             op0=ALU.bitwise_and))
    p.phase()
    p.on("vector", lambda e: e.tensor_scalar(pre2t, ebmt, qc["CLAMPL"], qc["CLAMPH"],
                                             op0=ALU.max, op1=ALU.min))
    p.phase()
    p.on("vector", lambda e: e.tensor_scalar(ebmt, pre2t, qc["PREC"], -1,
                                             op0=ALU.subtract, op1=ALU.mult))
    p.phase()
    p.on("gpsimd", lambda e: e.tensor_tensor(ubt.bitcast(I32), s0t.bitcast(I32), ebmt,
                                             op=ALU.add))
    p.phase()
    p.on("scalar", lambda e: e.activation(iqt, ubt, AF.Copy))
    ubt_i = ubt.bitcast(I32)
    clipf = iqt.bitcast(F32)
    p.phase()
    p.on("vector", lambda e: e.tensor_tensor(errt, ubt, iqt, op=ALU.subtract))
    p.phase()
    p.on("vector", lambda e: e.tensor_scalar(ubt, errt, 0.5, None, op0=ALU.is_gt))
    p.phase()
    p.on("vector", lambda e: e.scalar_tensor_tensor(s0t, errt, -0.5, ubt,
                                                    op0=ALU.is_lt, op1=ALU.subtract))
    p.phase()
    p.on("vector", lambda e: e.tensor_tensor(ubt_i, iqt, s0t, op=ALU.subtract))
    p.phase()
    p.on("vector", lambda e: e.tensor_tensor(s0t, pre2t.bitcast(F32), ubt_i, op=ALU.mult))
    p.phase()
    p.on("vector", lambda e: e.tensor_scalar(clipf, s0t, qc["C"], -qc["C"],
                                             op0=ALU.min, op1=ALU.max))
    p.phase()
    p.on("vector", lambda e: e.tensor_tensor(out3, clipf.rearrange("p (b c) -> p b c", c=32), sbxb, op=ALU.mult))



def build_l1(debug=False):
    nc = bass.Bass("TRN2", target_bir_lowering=False)
    x1f = nc.dram_tensor("x1f", [T, D], F32, kind="ExternalInput").ap()
    x1o = nc.dram_tensor("x1o", [NOWN, D], F32, kind="ExternalInput").ap()
    x1ot = nc.dram_tensor("x1ot", [D, NOWN], F32, kind="ExternalInput").ap()
    wq = nc.dram_tensor("wq", [D, D], F32, kind="ExternalInput").ap()
    wo1 = nc.dram_tensor("wo1", [D, D], F32, kind="ExternalInput").ap()
    wk = nc.dram_tensor("wk", [D, 512], F32, kind="ExternalInput").ap()
    wv = nc.dram_tensor("wv", [D, 512], F32, kind="ExternalInput").ap()
    wg = nc.dram_tensor("wg", [I_FFN, D], F32, kind="ExternalInput").ap()
    wu = nc.dram_tensor("wu", [I_FFN, D], F32, kind="ExternalInput").ap()
    wd = nc.dram_tensor("wd", [D, I_FFN], F32, kind="ExternalInput").ap()
    cosq = nc.dram_tensor("cosq", [64, NOWN], F32, kind="ExternalInput").ap()
    sinq = nc.dram_tensor("sinq", [64, NOWN], F32, kind="ExternalInput").ap()
    cosk = nc.dram_tensor("cosk", [64, T], F32, kind="ExternalInput").ap()
    sink = nc.dram_tensor("sink", [64, T], F32, kind="ExternalInput").ap()
    aliq = nc.dram_tensor("aliq", [128, NH * 32], F32, kind="ExternalInput").ap()
    mskt = nc.dram_tensor("mskt", [128, 128], F32, kind="ExternalInput").ap()
    emsd = nc.dram_tensor("emsd", [128, 512], BF16, kind="ExternalInput").ap()
    idb = nc.dram_tensor("idb", [128, 128], BF16, kind="ExternalInput").ap()
    onesd = nc.dram_tensor("onesd", [128, 1], BF16, kind="ExternalInput").ap()
    outT = nc.dram_tensor("outT", [D, NOWN], F32, kind="ExternalOutput").ap()
    wqsp = nc.dram_tensor("wqsp", [D, D], BF16, kind="Internal").ap()
    wo1sp = nc.dram_tensor("wo1sp", [D, D], BF16, kind="Internal").ap()
    ktsp = nc.dram_tensor("ktsp", [512, T], BF16, kind="Internal").ap()
    vsp = nc.dram_tensor("vsp", [T, 512], BF16, kind="Internal").ap()
    wgtsp = nc.dram_tensor("wgtsp", [D, I_FFN], BF16, kind="Internal").ap()
    wutsp = nc.dram_tensor("wutsp", [D, I_FFN], BF16, kind="Internal").ap()
    wdtsp = nc.dram_tensor("wdtsp", [I_FFN, D], BF16, kind="Internal").ap()
    s1tsp = nc.dram_tensor("s1tsp", [D, NOWN], BF16, kind="Internal").ap()
    if debug:
        dbg_xq = nc.dram_tensor("dbg_xq", [128, 16384], BF16, kind="ExternalOutput").ap()
        dbg_it = nc.dram_tensor("dbg_it", [128, 32768], BF16, kind="ExternalOutput").ap()
        dbg_g = nc.dram_tensor("dbg_g", [128, 2048], F32, kind="ExternalOutput").ap()

    p = Prog(nc)
    sb = p.sb
    ident = sb("ident", [128, 128], BF16)
    emsk = sb("emsk", [128, 512], BF16)
    onesb = sb("onesb", [128, 1], BF16)
    aliqt = sb("aliqt", [128, NH * 32], F32)
    cosqt = sb("cosqt", [128, NOWN], F32)
    poolI = sb("poolI", [128, 32768], BF16)
    xqTp = sb("xqTp", [128, 16384], BF16)
    workB = sb("workB", [128, 8192], I32)
    qtmp = sb("qtmp", [128, 8192], I32)
    xt = sb("xt", [128, D], F32)
    scr = sb("scr", [128, D], F32)
    xnb = sb("xnb", [128, D], BF16)
    s1 = sb("s1", [128, D], F32)
    attn_s = sb("attn_s", [128, D], BF16)
    stat = sb("stat", [128, 32], F32)
    blkf = sb("blkf", [128, 256], F32)
    blki = sb("blki", [128, 4 * 256], I32)
    poolF = poolI.bitcast(F32)
    poolI32 = poolI.bitcast(I32)
    workBF = workB.bitcast(F32)
    workBB = workB.bitcast(BF16)
    qtmpF = qtmp.bitcast(F32)
    qtmpB = qtmp.bitcast(BF16)
    xqF = xqTp.bitcast(F32)
    xnT = workB.bitcast(BF16)[:, 4096:8192]
    kst = attn_s[:, 0:512]
    pswide = p.psum("pswide", [128, 2048], F32)
    pt0 = p.psum("pt0", [128, 2048], BF16)
    pt1 = p.psum("pt1", [128, 2048], BF16)

    def transpose_pair(src_bf, dst_bf, nchunks, src_off=0, dst_off=0):
        done = 0
        while done < nchunks:
            cnt = min(16, nchunks - done)
            pt = pt0 if ((done // 16) % 2 == 0) else pt1
            p.phase()
            def t1(e, done=done, cnt=cnt, pt=pt):
                r = None
                for j in range(cnt):
                    r = e.transpose(pt[:, j * 128:(j + 1) * 128],
                                    src_bf[:, src_off + (done + j) * 128: src_off + (done + j + 1) * 128],
                                    ident[:])
                return r
            p.on("tensor", t1)
            p.phase()
            def c1(e, done=done, cnt=cnt, pt=pt):
                return e.tensor_copy(
                    dst_bf[:, dst_off + done * 128: dst_off + (done + cnt) * 128],
                    pt[:, 0:cnt * 128])
            p.on("vector", c1)
            done += cnt

    p.phase()
    p.dma("sync", ident[:], idb)
    p.dma("sync", emsk[:], emsd)
    p.dma("sync", onesb[:], onesd)
    p.dma("sync", aliqt[:], aliq)
    p.dma("sync", cosqt[0:64, :], cosq)
    p.dma("sync", cosqt[64:128, :], sinq)
    for (srcw, dstsp) in ((wq, wqsp), (wo1, wo1sp)):
        for hc in range(16):
            p.phase()
            p.dma("sync", xt[:], srcw[hc * 128:(hc + 1) * 128, :])
            p.phase()
            p.on("vector", lambda e: e.tensor_copy(xnb[:], xt[:]))
            p.phase()
            p.dma("scalar", dstsp[hc * 128:(hc + 1) * 128, :], xnb[:])
    s0w = poolF[:, 0:8192]
    ubw = poolF[:, 8192:16384]
    ebmw = workB[:, 0:8192]
    pre2w = qtmp[:, 0:8192]
    iqw = workB[:, 0:8192]
    qbfw = xqTp[:, 8192:16384]
    qtrw = poolI[:, 0:8192]
    errw = xqTp.bitcast(F32)[:, 0:8192]
    for wsrc, wdst, nrow, ncol in ((wg, wgtsp, I_FFN, D), (wu, wutsp, I_FFN, D),
                                   (wd, wdtsp, D, I_FFN)):
        rows_per_wide = 8192 // ncol
        nwide = nrow // (128 * rows_per_wide)
        for wI in range(nwide):
            p.phase()
            for j in range(rows_per_wide):
                r0 = (wI * rows_per_wide + j) * 128
                p.dma("sync", ubw[:, j * ncol:(j + 1) * ncol], wsrc[r0:r0 + 128, :])
            emit_quant(p, ubw, qbfw, 8192, QW, s0w, ebmw, pre2w, iqw,
                       ubw, blkf[:, 0:256], blki[:, 0:256],
                       blki[:, 256:512], blki[:, 512:768], errw, blki[:, 768:1024])
            transpose_pair(qbfw, qtrw, 64, 0, 0)
            p.phase()
            for j in range(rows_per_wide):
                for cc in range(ncol // 128):
                    src_c = (j * (ncol // 128) + cc) * 128
                    r0 = (wI * rows_per_wide + j) * 128
                    p.dma("scalar", wdst[cc * 128:(cc + 1) * 128, r0:r0 + 128],
                          qtrw[:, src_c:src_c + 128])

    coskt = poolF[:, 0:4096]
    sinkt = poolF[:, 4096:8192]
    wkb = xqTp[:, 0:8192]
    wvb = xqTp[:, 8192:16384]
    p.phase()
    p.dma("sync", coskt[0:64, :], cosk)
    p.dma("sync", sinkt[0:64, :], sink)
    for name, srcw, dstb in (("wk", wk, wkb), ("wv", wv, wvb)):
        for hc in range(16):
            p.phase()
            p.dma("sync", xt[:, 0:512], srcw[hc * 128:(hc + 1) * 128, :])
            p.phase()
            p.on("vector", (lambda e, hc=hc, dstb=dstb: e.tensor_copy(
                dstb[:, hc * 512:(hc + 1) * 512], xt[:, 0:512])))
    for t in range(NT):
        p.phase()
        p.dma("sync", xt[:], x1f[t * 128:(t + 1) * 128, :])
        emit_ln(p, xt[:], scr[:], stat, n=D, want_bf=xnb[:], want_f32=None)
        transpose_pair(xnb[:], xnT, 16)
        p.phase()
        def mm_k(e):
            r = None
            for oc in range(4):
                for hc in range(16):
                    r = e.matmul(pswide[:, oc * 128:(oc + 1) * 128],
                                 wkb[:, hc * 512 + oc * 128: hc * 512 + (oc + 1) * 128],
                                 xnT[:, hc * 128:(hc + 1) * 128],
                                 start=(hc == 0), stop=(hc == 15))
            return r
        p.on("tensor", mm_k)
        p.phase()
        def rope_k1(e, t=t):
            r = None
            for oc in range(4):
                ps = pswide[:, oc * 128:(oc + 1) * 128]
                cs = coskt[0:64, t * 128:(t + 1) * 128]
                sn = sinkt[0:64, t * 128:(t + 1) * 128]
                e.tensor_tensor(scr[0:64, oc * 512 + 0:oc * 512 + 128], ps[0:64, :], cs, op=ALU.mult)
                e.tensor_tensor(scr[0:64, oc * 512 + 128:oc * 512 + 256], ps[64:128, :], sn, op=ALU.mult)
                e.tensor_tensor(scr[0:64, oc * 512 + 256:oc * 512 + 384], ps[0:64, :], sn, op=ALU.mult)
                r = e.tensor_tensor(scr[0:64, oc * 512 + 384:oc * 512 + 512], ps[64:128, :], cs, op=ALU.mult)
            return r
        p.on("vector", rope_k1)
        p.phase()
        def rope_k2(e, t=t):
            r = None
            for oc in range(4):
                e.tensor_tensor(kst[0:64, oc * 128:(oc + 1) * 128],
                                scr[0:64, oc * 512 + 0:oc * 512 + 128],
                                scr[0:64, oc * 512 + 128:oc * 512 + 256], op=ALU.subtract)
                r = e.tensor_tensor(kst[64:128, oc * 128:(oc + 1) * 128],
                                    scr[0:64, oc * 512 + 256:oc * 512 + 384],
                                    scr[0:64, oc * 512 + 384:oc * 512 + 512], op=ALU.add)
            return r
        p.on("vector", rope_k2)
        p.phase()
        for oc in range(4):
            p.dma("sync", ktsp[oc * 128:(oc + 1) * 128, t * 128:(t + 1) * 128],
                  kst[:, oc * 128:(oc + 1) * 128])
        p.phase()
        def mm_v(e):
            r = None
            for hc in range(16):
                r = e.matmul(pswide[:, 512:1024],
                             xnT[:, hc * 128:(hc + 1) * 128],
                             wvb[:, hc * 512:(hc + 1) * 512],
                             start=(hc == 0), stop=(hc == 15))
            return r
        p.on("tensor", mm_v)
        p.phase()
        p.on("vector", lambda e: e.tensor_copy(kst[:], pswide[:, 512:1024]))
        p.phase()
        p.dma("sync", vsp[t * 128:(t + 1) * 128, :], kst[:])

    xnoT = poolI[:, 16384:32768]
    qTall = poolI[:, 0:16384]
    ebma = workB[:, 0:2048]
    pre2a = qtmp[:, 0:2048]
    iqa = workB[:, 2048:4096]
    for s in range(NSLOT):
        p.phase()
        p.dma("sync", xt[:], x1o[s * 128:(s + 1) * 128, :])
        emit_ln(p, xt[:], scr[:], stat, n=D, want_bf=xnb[:], want_f32=s1[:])
        transpose_pair(xnb[:], xnoT, 16, 0, s * 2048)
        emit_quant(p, s1[:], xnb[:], D, QA, scr[:], ebma, pre2a, iqa,
                   s1[:], blkf[:, 0:64], blki[:, 0:64],
                   blki[:, 256:320], blki[:, 512:576], xt[:], blki[:, 768:832])

        transpose_pair(xnb[:], xqTp[:, 0:16384], 16, 0, s * 2048)
    for oc in range(16):
        p.phase()
        for hc in range(16):
            p.dma("sync", qtmpB[:, hc * 128:(hc + 1) * 128],
                  wqsp[hc * 128:(hc + 1) * 128, oc * 128:(oc + 1) * 128])
        p.phase()
        def mm_q(e, oc=oc):
            r = None
            for si in range(NSLOT):
                for hc in range(16):
                    r = e.matmul(pswide[:, si * 128:(si + 1) * 128],
                                 qtmpB[:, hc * 128:(hc + 1) * 128],
                                 xnoT[:, si * 2048 + hc * 128: si * 2048 + (hc + 1) * 128],
                                 start=(hc == 0), stop=(hc == 15))
            return r
        p.on("tensor", mm_q)
        p.phase()
        def rope_q1(e, oc=oc):
            r = None
            for si in range(NSLOT):
                ps = pswide[:, si * 128:(si + 1) * 128]
                cs = cosqt[0:64, si * 128:(si + 1) * 128]
                sn = cosqt[64:128, si * 128:(si + 1) * 128]
                e.tensor_tensor(scr[0:64, si * 256 + 0:si * 256 + 128], ps[0:64, :], cs, op=ALU.mult)
                r = e.tensor_tensor(scr[0:64, si * 256 + 128:si * 256 + 256], ps[64:128, :], sn, op=ALU.mult)
            return r
        p.on("vector", rope_q1)
        p.phase()
        def rope_q2(e, oc=oc):
            r = None
            for si in range(NSLOT):
                ps = pswide[:, si * 128:(si + 1) * 128]
                cs = cosqt[0:64, si * 128:(si + 1) * 128]
                sn = cosqt[64:128, si * 128:(si + 1) * 128]
                e.tensor_tensor(qTall[0:64, oc * 1024 + si * 128: oc * 1024 + (si + 1) * 128],
                                scr[0:64, si * 256 + 0:si * 256 + 128],
                                scr[0:64, si * 256 + 128:si * 256 + 256], op=ALU.subtract)
                e.tensor_tensor(xnb.bitcast(F32)[0:64, si * 128:(si + 1) * 128], ps[0:64, :], sn, op=ALU.mult)
                r = e.tensor_tensor(s1[0:64, si * 128:(si + 1) * 128], ps[64:128, :], cs, op=ALU.mult)
            return r
        p.on("vector", rope_q2)
        p.phase()
        def rope_q3(e, oc=oc):
            r = None
            for si in range(NSLOT):
                r = e.tensor_tensor(qTall[64:128, oc * 1024 + si * 128: oc * 1024 + (si + 1) * 128],
                                    xnb.bitcast(F32)[0:64, si * 128:(si + 1) * 128],
                                    s1[0:64, si * 128:(si + 1) * 128], op=ALU.add)
            return r
        p.on("vector", rope_q3)

    if debug:
        p.phase()
        p.dma("sync", dbg_xq, xqTp[:])
    pTb = workBB[:, 0:4096]
    kstr = workBB[:, 4096:8192]
    vstr = workBB[:, 8192:12288]
    s1b = workBB[:, 12288:16384]
    for s in range(NSLOT):
        nkt = 4 * (s + 1)
        for hg in range(4):
            p.phase()
            for kt in range(nkt):
                p.dma("sync", kstr[:, kt * 128:(kt + 1) * 128],
                      ktsp[hg * 128:(hg + 1) * 128, kt * 128:(kt + 1) * 128])
                p.dma("scalar", vstr[:, kt * 128:(kt + 1) * 128],
                      vsp[kt * 128:(kt + 1) * 128, hg * 128:(hg + 1) * 128])
            for hh in range(4):
                head = hg * 4 + hh
                done = 0
                while done < nkt:
                    cnt = min(16, nkt - done)
                    p.phase()
                    def mm_s(e, done=done, cnt=cnt, head=head, s=s):
                        r = None
                        for j in range(cnt):
                            kt = done + j
                            r = e.matmul(pswide[:, j * 128:(j + 1) * 128],
                                         kstr[:, kt * 128:(kt + 1) * 128],
                                         qTall[:, head * 1024 + s * 128: head * 1024 + (s + 1) * 128],
                                         start=True, stop=True)
                        return r
                    p.on("tensor", mm_s)
                    p.phase()
                    def ev_s(e, done=done, cnt=cnt, head=head, s=s):
                        r = None
                        for j in range(cnt):
                            kt = done + j
                            col = head * 32 + (4 * s + 3 - kt)
                            r = e.activation(pTb[:, kt * 128:(kt + 1) * 128],
                                             pswide[:, j * 128:(j + 1) * 128],
                                             AF.Exp, bias=aliqt[:, col:col + 1])
                        return r
                    p.on("scalar", ev_s)
                    done += cnt
                p.phase()
                def mask_last(e, head=head, s=s, nkt=nkt):
                    r = None
                    for j in range(4):
                        kt = nkt - 4 + j
                        r = e.tensor_tensor(
                            pTb[:, kt * 128:(kt + 1) * 128],
                            pTb[:, kt * 128:(kt + 1) * 128],
                            emsk[:, j * 128:(j + 1) * 128], op=ALU.mult)
                    return r
                p.on("vector", mask_last)
                p.phase()
                def mm_sv(e, head=head, nkt=nkt, hh=hh):
                    r = None
                    for kt in range(nkt):
                        e.matmul(pswide[:, 0:128],
                                 pTb[:, kt * 128:(kt + 1) * 128],
                                 vstr[:, kt * 128:(kt + 1) * 128],
                                 start=(kt == 0), stop=(kt == nkt - 1))
                        r = e.matmul(pswide[:, 512:513],
                                     pTb[:, kt * 128:(kt + 1) * 128],
                                     onesb[:],
                                     start=(kt == 0), stop=(kt == nkt - 1))
                    return r
                p.on("tensor", mm_sv)
                p.phase()
                p.on("vector", (lambda e, hh=hh: e.reciprocal(
                    stat[:, 16 + hh:17 + hh], pswide[:, 512:513])))
                p.phase()
                p.on("vector", (lambda e, hh=hh, head=head: e.tensor_scalar(
                    attn_s[:, head * 128:(head + 1) * 128],
                    pswide[:, 0:128],
                    stat[:, 16 + hh:17 + hh], None, op0=ALU.mult)))
        transpose_pair(attn_s[:], xnT, 16)
        for ob in range(4):
            p.phase()
            for ic in range(16):
                p.dma("sync", qtmpB[:, ic * 512:(ic + 1) * 512],
                      wo1sp[ic * 128:(ic + 1) * 128, ob * 512:(ob + 1) * 512])
            p.phase()
            def mm_wo(e, ob=ob):
                r = None
                for ic in range(16):
                    r = e.matmul(pswide[:, 0:512], xnT[:, ic * 128:(ic + 1) * 128],
                                 qtmpB[:, ic * 512:(ic + 1) * 512],
                                 start=(ic == 0), stop=(ic == 15))
                return r
            p.on("tensor", mm_wo)
            p.phase()
            p.on("vector", (lambda e, ob=ob: e.tensor_copy(
                s1[:, ob * 512:(ob + 1) * 512], pswide[:, 0:512])))
        p.phase()
        p.on("vector", lambda e: e.tensor_copy(s1b[:, 0:2048], s1[:]))
        transpose_pair(s1b[:, 0:2048], xnT, 16)
        p.phase()
        for oc in range(16):
            p.dma("scalar", s1tsp[oc * 128:(oc + 1) * 128, s * 128:(s + 1) * 128],
                  xnT[:, oc * 128:(oc + 1) * 128])

    interT = poolI[:, 0:32768]
    wstr1 = qtmpB[:, 0:8192]
    wstr2 = workBB[:, 0:8192]
    gbuf = s1[:]
    sgbuf = xt[:]
    for g2 in range(2):
        for ob in range(16):
            p.phase()
            for hc in range(16):
                p.dma("sync", wstr1[:, hc * 512:(hc + 1) * 512],
                      wgtsp[hc * 128:(hc + 1) * 128, ob * 512:(ob + 1) * 512])
                p.dma("scalar", wstr2[:, hc * 512:(hc + 1) * 512],
                      wutsp[hc * 128:(hc + 1) * 128, ob * 512:(ob + 1) * 512])
            p.phase()
            def mm_g(e, g2=g2, ob=ob):
                r = None
                for ti in range(4):
                    si = g2 * 4 + ti
                    for hc in range(16):
                        r = e.matmul(pswide[:, ti * 512:(ti + 1) * 512],
                                     xqTp[:, si * 2048 + hc * 128: si * 2048 + (hc + 1) * 128],
                                     wstr1[:, hc * 512:(hc + 1) * 512],
                                     start=(hc == 0), stop=(hc == 15))
                return r
            p.on("tensor", mm_g)
            p.phase()
            p.on("vector", lambda e: e.tensor_copy(gbuf[:], pswide[:]))
            p.phase()
            p.on("scalar", lambda e: e.activation(sgbuf[:], gbuf[:], AF.Silu))
            p.phase()
            def mm_u(e, g2=g2, ob=ob):
                r = None
                for ti in range(4):
                    si = g2 * 4 + ti
                    for hc in range(16):
                        r = e.matmul(pswide[:, ti * 512:(ti + 1) * 512],
                                     xqTp[:, si * 2048 + hc * 128: si * 2048 + (hc + 1) * 128],
                                     wstr2[:, hc * 512:(hc + 1) * 512],
                                     start=(hc == 0), stop=(hc == 15))
                return r
            p.on("tensor", mm_u)
            p.phase()
            p.on("vector", lambda e: e.tensor_tensor(scr[:], sgbuf[:], pswide[:], op=ALU.mult))
            emit_quant(p, scr[:], xnb[:], D, QA, gbuf[:, 0:2048], workB[:, 0:2048],
                       qtmp[:, 0:2048], workB[:, 2048:4096], sgbuf[:, 0:2048],
                       blkf[:, 0:64], blki[:, 0:64], blki[:, 256:320],
                       blki[:, 512:576], workB.bitcast(F32)[:, 4096:6144], blki[:, 768:832])
            p.phase()
            def tq(e, ob=ob):
                r = None
                for j in range(16):
                    r = e.transpose(pt0[:, j * 128:(j + 1) * 128],
                                    xnb[:, j * 128:(j + 1) * 128], ident[:])
                return r
            p.on("tensor", tq)
            p.phase()
            def cq(e, ob=ob):
                r = None
                for ti in range(4):
                    for cc in range(4):
                        ic = ob * 4 + cc
                        src = (ti * 4 + cc) * 128
                        r = e.tensor_copy(interT[:, ic * 512 + ti * 128: ic * 512 + (ti + 1) * 128],
                                          pt0[:, src:src + 128])
                return r
            p.on("vector", cq)
        if debug and g2 == 1:
            p.phase()
            p.dma("sync", dbg_it, interT)
        for oc2 in range(16):
            p.phase()
            for ic in range(64):
                p.dma("sync", wstr1[:, ic * 128:(ic + 1) * 128],
                      wdtsp[ic * 128:(ic + 1) * 128, oc2 * 128:(oc2 + 1) * 128])
            p.phase()
            def mm_d(e, oc2=oc2):
                r = None
                for ic in range(64):
                    r = e.matmul(pswide[:, 0:512],
                                 wstr1[:, ic * 128:(ic + 1) * 128],
                                 interT[:, ic * 512:(ic + 1) * 512],
                                 start=(ic == 0), stop=(ic == 63))
                return r
            p.on("tensor", mm_d)
            p.phase()
            p.dma("scalar", kstr[:, 0:512],
                  s1tsp[oc2 * 128:(oc2 + 1) * 128, g2 * 512:(g2 + 1) * 512])
            p.dma("sync", scr[:, 0:512], x1ot[oc2 * 128:(oc2 + 1) * 128, g2 * 512:(g2 + 1) * 512])
            p.on("scalar", lambda e: e.activation(gbuf[:, 0:512], pswide[:, 0:512], AF.Silu))
            p.phase()
            p.on("vector", lambda e: e.tensor_tensor(gbuf[:, 512:1024], gbuf[:, 0:512],
                                                     kstr[:, 0:512], op=ALU.mult))
            p.phase()
            p.on("gpsimd", lambda e: e.tensor_tensor(gbuf[:, 1024:1536], gbuf[:, 512:1024],
                                                     scr[:, 0:512], op=ALU.add))
            p.phase()
            p.dma("sync", outT[oc2 * 128:(oc2 + 1) * 128, g2 * 512:(g2 + 1) * 512],
                  gbuf[:, 1024:1536])

    p.phase()
    p.finalize()
    return nc



def _tile128(v):
    return np.ascontiguousarray(np.broadcast_to(np.asarray(v, np.float32)[None, :], (128, v.shape[-1])))


def own_rows(m):
    ids = []
    for s in range(NSLOT):
        t0 = (4 * s + m) * 128
        ids.extend(range(t0, t0 + 128))
    return np.array(ids)


def l0_inmaps(inputs):
    x = np.asarray(inputs["L0_attn_x"], np.float32)
    ws = np.ascontiguousarray(np.asarray(inputs["L0_attn_Ws"], np.float32))
    wa = np.ascontiguousarray(np.asarray(inputs["L0_attn_Wa"], np.float32))
    wot = np.ascontiguousarray(np.asarray(inputs["L0_attn_Wo"], np.float32).T)
    fw = np.ascontiguousarray(np.asarray(inputs["L0_ffn_weight"], np.float32))
    gam = _tile128(np.asarray(inputs["L0_ffn_gamma"], np.float32))
    bet = _tile128(np.asarray(inputs["L0_ffn_beta"], np.float32))
    bob = _tile128(np.asarray(inputs["L0_attn_bo"], np.float32))
    fbb = _tile128(np.asarray(inputs["L0_ffn_bias"], np.float32))
    alb = _tile128(np.asarray(inputs["L0_ffn_alpha"], np.float32))
    idb = np.eye(128).astype(BF)
    kpo = np.ascontiguousarray(np.broadcast_to(
        np.arange(512, dtype=np.float32)[None, :], (128, 512)))
    maps = []
    for c in range(NCORES):
        b, m = c // 4, c % 4
        rows = own_rows(m)
        qpo = (128 * m + np.arange(128, dtype=np.float32))[:, None].copy()
        maps.append({
            "xf": np.ascontiguousarray(x[b]),
            "xo": np.ascontiguousarray(x[b][rows]),
            "ws": ws, "wa": wa, "wot": wot, "fw": fw,
            "gam": gam, "bet": bet, "bob": bob, "fbb": fbb, "alb": alb,
            "idb": idb, "kpo": kpo, "qpo": qpo,
        })
    return maps


def assemble_x1(results):
    x1 = np.empty((B, T, D), np.float32)
    for c in range(NCORES):
        b, m = c // 4, c % 4
        x1[b, own_rows(m), :] = results[c]["x1o"]
    return x1




def l1_inmaps(inputs, x1):
    wq = np.ascontiguousarray(np.asarray(inputs["L1_attn_Wq"], np.float32))
    wk = np.ascontiguousarray(np.asarray(inputs["L1_attn_Wk"], np.float32))
    wv = np.ascontiguousarray(np.asarray(inputs["L1_attn_Wv"], np.float32))
    wo1 = np.ascontiguousarray(np.asarray(inputs["L1_attn_Wo"], np.float32))
    wg = np.ascontiguousarray(np.asarray(inputs["L1_ffn_w_gate"], np.float32))
    wu = np.ascontiguousarray(np.asarray(inputs["L1_ffn_w_up"], np.float32))
    wd = np.ascontiguousarray(np.asarray(inputs["L1_ffn_w_down"], np.float32))
    idb = np.eye(128).astype(BF)
    onesd = np.ones((128, 1), BF)
    mskt = np.zeros((128, 128), np.float32)
    half = HD // 2
    theta = 1.0 / (10000.0 ** (np.arange(half, dtype=np.float64) / half))
    posf = np.arange(T, dtype=np.float64)
    frq = np.outer(posf, theta)
    cosk = np.ascontiguousarray(np.cos(frq).T.astype(np.float32))
    sink = np.ascontiguousarray(np.sin(frq).T.astype(np.float32))
    sigma = 1.0 / np.sqrt(float(HD))
    slopes = 2.0 ** (-8.0 * np.arange(1, NH + 1, dtype=np.float64) / NH)
    maps = []
    for c in range(NCORES):
        b, m = c // 4, c % 4
        rows = own_rows(m)
        x1b = np.ascontiguousarray(x1[b])
        x1own = np.ascontiguousarray(x1[b][rows])
        x1ot = np.ascontiguousarray(x1own.T)
        frq_o = frq[rows]
        cosq = np.ascontiguousarray((np.cos(frq_o) * sigma).T.astype(np.float32))
        sinq = np.ascontiguousarray((np.sin(frq_o) * sigma).T.astype(np.float32))
        aliq = np.full((128, NH * 32), -1.0e30, np.float32)
        pcol = np.arange(128, dtype=np.float64)
        for h in range(NH):
            for io in range(32):
                if io < 3 - m:
                    continue
                col = h * 32 + io
                aliq[:, col] = (slopes[h] * (pcol - 64.0 - 128.0 * (io + m - 3))).astype(np.float32)
        emsk = np.ones((128, 512), np.float32)
        jj = np.arange(128)
        emsk[:, m * 128:(m + 1) * 128] = (pcol[:, None] <= jj[None, :])
        emsk[:, (m + 1) * 128:] = 0.0
        maps.append({
            "x1f": x1b, "x1o": x1own, "x1ot": x1ot,
            "wq": wq, "wo1": wo1, "wk": wk, "wv": wv,
            "wg": wg, "wu": wu, "wd": wd,
            "cosq": cosq, "sinq": sinq, "cosk": cosk, "sink": sink,
            "aliq": aliq, "mskt": mskt, "emsd": emsk.astype(BF),
            "idb": idb, "onesd": onesd,
        })
    return maps


def assemble_out(results):
    out = np.empty((B, T, D), np.float32)
    for c in range(NCORES):
        b, m = c // 4, c % 4
        out[b, own_rows(m), :] = results[c]["outT"].T
    return out


def kernel(**inputs):
    nc0 = build_l0()
    res0 = run_bass_kernel_spmd(nc0, l0_inmaps(inputs), core_ids=list(range(NCORES)))
    x1 = assemble_x1(res0.results)
    nc1 = build_l1()
    res1 = run_bass_kernel_spmd(nc1, l1_inmaps(inputs, x1), core_ids=list(range(NCORES)))
    return assemble_out(res1.results)
